# version 01
# baseline (speedup 1.0000x reference)
import contextlib
import numpy as np
import ml_dtypes
import concourse.bass as bass
import concourse.mybir as mybir
from concourse.bass_utils import run_bass_kernel_spmd

F32 = mybir.dt.float32
BF16 = mybir.dt.bfloat16
U8 = mybir.dt.uint8
I32 = mybir.dt.int32
AF = mybir.ActivationFunctionType
ALU = mybir.AluOpType
AX = mybir.AxisListType
BF = ml_dtypes.bfloat16

NCORES = 8
B, S, D = 4, 8192, 1024
TC = 4096
SCALE1 = 128 ** -0.5
DEQ = 1.0 / 127.0

_CACHE = {}


def _cached(name, fn):
    if name not in _CACHE:
        _CACHE[name] = fn()
    return _CACHE[name]


class Sch:

    def __init__(self, nc, st):
        self.nc = nc
        self.sem = {}
        self.cnt = {}
        self.lastw = {}
        for n in ("t", "v", "s", "g", "q"):
            self.sem[n] = st.enter_context(nc.semaphore("sem_" + n))
            self.cnt[n] = 0
        self.eng = {"t": nc.tensor, "v": nc.vector, "s": nc.scalar,
                    "g": nc.gpsimd, "q": nc.sync}

    def _op(self, name, inst, inc):
        inst.then_inc(self.sem[name], inc)
        self.cnt[name] += inc
        return (name, self.cnt[name])

    def T(self, inst):
        return self._op("t", inst, 1)

    def V(self, inst):
        return self._op("v", inst, 1)

    def Vb(self, inst):
        tok = self._op("v", inst, 1)
        self.eng["v"].wait_ge(self.sem["v"], tok[1])
        return tok

    def Sb(self, inst):
        tok = self._op("s", inst, 1)
        self.eng["s"].wait_ge(self.sem["s"], tok[1])
        return tok

    def S(self, inst):
        return self._op("s", inst, 1)

    def G(self, inst):
        return self._op("g", inst, 1)

    def ld(self, dst, src):
        return self._op("q", self.nc.sync.dma_start(out=dst, in_=src), 16)

    def st(self, dst, src):
        return self._op("g", self.nc.gpsimd.dma_start(out=dst, in_=src), 16)

    def wait(self, name, tok):
        pn, v = tok
        if v <= 0:
            return
        key = (name, pn)
        if self.lastw.get(key, -1) >= v:
            return
        self.eng[name].wait_ge(self.sem[pn], v)
        self.lastw[key] = v

    def wa(self, name):
        for pn in ("t", "v", "s", "g", "q"):
            if pn != name:
                self.wait(name, (pn, self.cnt[pn]))

    def finish(self):
        for pn in ("t", "v", "s", "q"):
            self.wait("g", (pn, self.cnt[pn]))
        self.eng["g"].wait_ge(self.sem["g"], self.cnt["g"])


def _sb(nc, st, name, shape, dt):
    return st.enter_context(nc.sbuf_tensor(name, shape, dt))


def _ps(nc, st, name, shape, dt=F32):
    return st.enter_context(nc.psum_tensor(name, shape, dt))


def _ln_tile(S_, nc, x_tile, xn_out, scr, c1, c2, c3, eps_ap, width):
    S_.wa("s")
    S_.wa("v")
    S_.Vb(nc.vector.reduce_sum(c1, x_tile, AX.X))
    S_.S(nc.scalar.activation(scr, x_tile, AF.Square, accum_out=c2))
    S_.wa("v")
    S_.Vb(nc.vector.tensor_scalar_mul(c1, c1, 1.0 / width))
    S_.Vb(nc.vector.tensor_scalar_mul(c2, c2, 1.0 / width))
    S_.Vb(nc.vector.tensor_mul(c3, c1, c1))
    S_.Vb(nc.vector.tensor_sub(c2, c2, c3))
    S_.wa("s")
    S_.Sb(nc.scalar.activation(c2, c2, AF.Sqrt, bias=eps_ap))
    S_.wa("v")
    S_.Vb(nc.vector.reciprocal(c2, c2))
    S_.Vb(nc.vector.scalar_tensor_tensor(c3, c1, -1.0, c2, ALU.mult, ALU.mult))
    S_.V(nc.vector.tensor_scalar(xn_out, x_tile, c2, c3, ALU.mult, ALU.add))


def _tr_tm_to_fm(S_, nc, src4, dst8, p_tr, s_id, sp_width):
    nt = sp_width // 128
    for j in range(8):
        S_.wa("t")
        for i in range(nt):
            S_.T(nc.tensor.transpose(p_tr[:, i * 128:(i + 1) * 128],
                                     src4[i][:, j * 128:(j + 1) * 128],
                                     s_id[:, :]))
        S_.wa("v")
        S_.V(nc.vector.tensor_copy(dst8[j][:, 0:sp_width], p_tr[:, 0:sp_width]))


def _tr_fm_to_tm_store(S_, nc, src8, p_tr, s_stm, dst_hbm, t0, s_id, sp_width,
                       nf=8):
    for i in range(sp_width // 128):
        S_.wa("t")
        for oc in range(nf):
            S_.T(nc.tensor.transpose(p_tr[:, oc * 128:(oc + 1) * 128],
                                     src8[oc][:, i * 128:(i + 1) * 128],
                                     s_id[:, :]))
        S_.wa("v")
        S_.V(nc.vector.tensor_copy(s_stm[:, 0:nf * 128], p_tr[:, 0:nf * 128]))
        S_.wa("g")
        S_.st(dst_hbm[t0 + i * 128:t0 + (i + 1) * 128, 0:nf * 128],
              s_stm[:, 0:nf * 128])


def build_W():
    nc = bass.Bass("TRN2", target_bir_lowering=False)
    k0t = nc.dram_tensor("k0t", [128, 4096], U8, kind="ExternalInput").ap()
    k0b = nc.dram_tensor("k0b", [128, 4096], U8, kind="ExternalInput").ap()
    k1s = nc.dram_tensor("k1s", [512, 4096], U8, kind="ExternalInput").ap()
    k2s = nc.dram_tensor("k2s", [512, 1024], U8, kind="ExternalInput").ap()
    wgTs = nc.dram_tensor("wgTs", [128, 3072], I32, kind="ExternalInput").ap()
    wuTs = nc.dram_tensor("wuTs", [128, 3072], I32, kind="ExternalInput").ap()
    wdTs = nc.dram_tensor("wdTs", [384, 1024], I32, kind="ExternalInput").ap()
    w0cs = nc.dram_tensor("w0cs", [128, 4096], BF16, kind="ExternalOutput").ap()
    w1s = nc.dram_tensor("w1s", [512, 4096], BF16, kind="ExternalOutput").ap()
    w2s = nc.dram_tensor("w2s", [512, 1024], BF16, kind="ExternalOutput").ap()
    wgs = nc.dram_tensor("wgs", [128, 3072], BF16, kind="ExternalOutput").ap()
    wus = nc.dram_tensor("wus", [128, 3072], BF16, kind="ExternalOutput").ap()
    wds = nc.dram_tensor("wds", [384, 1024], BF16, kind="ExternalOutput").ap()

    with contextlib.ExitStack() as st:
        a8 = _sb(nc, st, "a8", [128, 4096], U8)
        b8 = _sb(nc, st, "b8", [128, 4096], U8)
        ai = _sb(nc, st, "ai", [128, 3072], I32)
        obf = _sb(nc, st, "obf", [128, 4096], BF16)
        st.enter_context(nc.Block())
        S_ = Sch(nc, st)

        def conv_u8(srcs, dst, rows, cols, sub):
            for i in range(rows // 128):
                r = slice(i * 128, (i + 1) * 128)
                S_.wa("q")
                S_.ld(a8[:, 0:cols], srcs[0][r, :])
                if sub:
                    S_.ld(b8[:, 0:cols], srcs[1][r, :])
                S_.wa("v")
                if sub:
                    S_.V(nc.vector.tensor_sub(obf[:, 0:cols], a8[:, 0:cols],
                                              b8[:, 0:cols]))
                else:
                    S_.V(nc.vector.tensor_scalar(obf[:, 0:cols], a8[:, 0:cols],
                                                 2.0, -1.0, ALU.mult, ALU.add))
                S_.wa("g")
                S_.st(dst[r, :], obf[:, 0:cols])

        def conv_i32(src, dst, rows, cols):
            for i in range(rows // 128):
                r = slice(i * 128, (i + 1) * 128)
                S_.wa("q")
                S_.ld(ai[:, 0:cols], src[r, :])
                S_.wa("v")
                S_.V(nc.vector.tensor_copy(obf[:, 0:cols], ai[:, 0:cols]))
                S_.wa("g")
                S_.st(dst[r, :], obf[:, 0:cols])

        conv_u8((k0t, k0b), w0cs, 128, 4096, True)
        conv_u8((k1s,), w1s, 512, 4096, False)
        conv_u8((k2s,), w2s, 512, 1024, False)
        conv_i32(wgTs, wgs, 128, 3072)
        conv_i32(wuTs, wus, 128, 3072)
        conv_i32(wdTs, wds, 384, 1024)
        S_.finish()
    return nc


def build_A():
    nc = bass.Bass("TRN2", target_bir_lowering=False)
    x_tm = nc.dram_tensor("x_tm", [TC, 1024], F32, kind="ExternalInput").ap()
    Wq = nc.dram_tensor("Wq", [1024, 512], F32, kind="ExternalInput").ap()
    Wk = nc.dram_tensor("Wk", [1024, 512], F32, kind="ExternalInput").ap()
    Wv = nc.dram_tensor("Wv", [1024, 512], F32, kind="ExternalInput").ap()
    colsA = nc.dram_tensor("colsA", [128, 16], F32, kind="ExternalInput").ap()
    ident = nc.dram_tensor("ident", [128, 128], BF16, kind="ExternalInput").ap()
    segs = nc.dram_tensor("segs", [128, 32], BF16, kind="ExternalInput").ap()
    Eg = nc.dram_tensor("Eg", [8, 512], BF16, kind="ExternalInput").ap()
    ones8 = nc.dram_tensor("ones8", [8, 9], BF16, kind="ExternalInput").ap()

    xn_fm = nc.dram_tensor("xn_fm", [1024, TC], BF16, kind="ExternalOutput").ap()
    q_fm = nc.dram_tensor("q_fm", [512, TC], BF16, kind="ExternalOutput").ap()
    u_fm = nc.dram_tensor("u_fm", [512, TC], F32, kind="ExternalOutput").ap()
    floc = nc.dram_tensor("floc", [128, 4], F32, kind="ExternalOutput").ap()

    SP, NSP = 512, TC // 512
    with contextlib.ExitStack() as st:
        s_x = [_sb(nc, st, f"s_x{i}", [128, 1024], F32) for i in range(4)]
        s_xn = [_sb(nc, st, f"s_xn{i}", [128, 1024], BF16) for i in range(4)]
        s_scr = _sb(nc, st, "s_scr", [128, 1024], F32)
        s_c = _sb(nc, st, "s_c", [128, 8], F32)
        s_xnf = [_sb(nc, st, f"s_xnf{j}", [128, SP], BF16) for j in range(8)]
        s_wq = [_sb(nc, st, f"s_wq{k}", [128, 512], BF16) for k in range(8)]
        s_wk = [_sb(nc, st, f"s_wk{k}", [128, 512], BF16) for k in range(8)]
        s_wv = [_sb(nc, st, f"s_wv{k}", [128, 512], BF16) for k in range(8)]
        s_wld = _sb(nc, st, "s_wld", [128, 512], F32)
        s_cols = _sb(nc, st, "s_cols", [128, 16], F32)
        s_dec = _sb(nc, st, "s_dec", [128, 8], F32)
        s_dbc = [_sb(nc, st, f"s_dbc{g}", [128, SP], F32) for g in range(4)]
        s_ones = _sb(nc, st, "s_ones", [128, SP], F32)
        s_id = _sb(nc, st, "s_id", [128, 128], BF16)
        s_segs = _sb(nc, st, "s_segs", [128, 32], BF16)
        s_Eg = _sb(nc, st, "s_Eg", [8, 512], BF16)
        s_o8 = _sb(nc, st, "s_o8", [8, 9], BF16)
        s_q = [_sb(nc, st, f"s_q{g}", [128, SP], BF16) for g in range(4)]
        s_k = [_sb(nc, st, f"s_k{g}", [128, SP], BF16) for g in range(4)]
        s_v = [_sb(nc, st, f"s_v{g}", [128, SP], BF16) for g in range(4)]
        s_qk = [_sb(nc, st, f"s_qk{g}", [128, SP], BF16) for g in range(4)]
        s_ed = _sb(nc, st, "s_ed", [8, SP], BF16)
        s_rs = _sb(nc, st, "s_rs", [1, SP], F32)
        s_rsb = _sb(nc, st, "s_rsb", [1, SP], BF16)
        s_at = _sb(nc, st, "s_at", [8, SP], BF16)
        s_u = [_sb(nc, st, f"s_u{g}", [128, SP], F32) for g in range(4)]
        s_stt = [_sb(nc, st, f"s_stt{g}", [128, SP], F32) for g in range(4)]
        s_fc = _sb(nc, st, "s_fc", [128, 4], F32)
        s_eps = _sb(nc, st, "s_eps", [128, 1], F32)
        p_tr = _ps(nc, st, "p_tr", [128, 1024], BF16)
        p_mm = _ps(nc, st, "p_mm", [128, 512], F32)
        p_dot = _ps(nc, st, "p_dot", [128, 512], F32)
        p_sum = _ps(nc, st, "p_sum", [128, 512], F32)
        p_rb8 = _ps(nc, st, "p_rb8", [128, 512], F32)
        p_abc = _ps(nc, st, "p_abc", [128, 512], F32)
        st.enter_context(nc.Block())
        S_ = Sch(nc, st)

        S_.ld(s_id[:, :], ident)
        S_.ld(s_segs[:, :], segs)
        S_.ld(s_Eg[:, :], Eg)
        S_.ld(s_o8[:, :], ones8)
        S_.ld(s_cols[:, :], colsA)
        for wap, dstl in ((Wq, s_wq), (Wk, s_wk), (Wv, s_wv)):
            for kc in range(8):
                S_.wa("q")
                S_.ld(s_wld[:, :], wap[kc * 128:(kc + 1) * 128, :])
                S_.wa("v")
                S_.V(nc.vector.tensor_copy(dstl[kc][:, :], s_wld[:, :]))
        S_.wa("v")
        S_.V(nc.vector.memset(s_ones[:, :], 1.0))
        S_.V(nc.vector.memset(s_fc[:, :], 0.0))
        S_.V(nc.vector.memset(s_eps[:, :], 1e-5))
        S_.wa("s")
        for g in range(4):
            S_.S(nc.scalar.activation(s_dec[:, g:g + 1], s_cols[:, 12 + g:13 + g],
                                      AF.Sigmoid))
        S_.wa("v")
        for g in range(4):
            S_.Vb(nc.vector.tensor_scalar(s_dec[:, 4 + g:5 + g], s_dec[:, g:g + 1],
                                          -1.0, 1.0, ALU.mult, ALU.add))
            S_.V(nc.vector.tensor_scalar_mul(s_dbc[g][:, :], s_ones[:, :],
                                             s_dec[:, g:g + 1]))

        for sp in range(NSP):
            t0 = sp * SP
            S_.wa("q")
            for i in range(4):
                S_.ld(s_x[i][:, :], x_tm[t0 + i * 128:t0 + (i + 1) * 128, :])
            S_.wa("v")
            S_.wa("s")
            for i in range(4):
                _ln_tile(S_, nc, s_x[i][:, :], s_xn[i][:, :], s_scr[:, :],
                         s_c[:, 0:1], s_c[:, 1:2], s_c[:, 2:3], s_eps[:, 0:1],
                         1024)
            _tr_tm_to_fm(S_, nc, s_xn, s_xnf, p_tr, s_id, SP)
            S_.wa("g")
            for j in range(8):
                S_.st(xn_fm[j * 128:(j + 1) * 128, t0:t0 + SP], s_xnf[j][:, :])
            for dstl, wl, bcol in ((s_q, s_wq, 0), (s_k, s_wk, 4), (s_v, s_wv, 8)):
                for g in range(4):
                    S_.wa("t")
                    for kc in range(8):
                        S_.T(nc.tensor.matmul(p_mm[:, :],
                                              wl[kc][:, g * 128:(g + 1) * 128],
                                              s_xnf[kc][:, :],
                                              start=(kc == 0), stop=(kc == 7)))
                    S_.wa("v")
                    S_.V(nc.vector.tensor_scalar(dstl[g][:, :], p_mm[:, :],
                                                 s_cols[:, bcol + g:bcol + g + 1],
                                                 None, ALU.add))
            S_.wa("g")
            for g in range(4):
                S_.st(q_fm[g * 128:(g + 1) * 128, t0:t0 + SP], s_q[g][:, :])
            S_.wa("v")
            for g in range(4):
                S_.V(nc.vector.tensor_mul(s_qk[g][:, :], s_q[g][:, :],
                                          s_k[g][:, :]))
            S_.wa("t")
            for g in range(4):
                S_.T(nc.tensor.matmul(p_dot[0:8, :], s_segs[:, g * 8:(g + 1) * 8],
                                      s_qk[g][:, :], start=(g == 0),
                                      stop=(g == 3)))
            S_.wa("s")
            S_.S(nc.scalar.activation(s_ed[:, :], p_dot[0:8, :], AF.Exp,
                                      scale=0.125))
            S_.wa("t")
            S_.T(nc.tensor.matmul(p_sum[0:1, :], s_o8[:, 0:1], s_ed[:, :],
                                  start=True, stop=True))
            S_.wa("v")
            S_.V(nc.vector.reciprocal(s_rs[:, :], p_sum[0:1, :]))
            S_.V(nc.vector.tensor_copy(s_rsb[:, :], s_rs[:, :]))
            S_.wa("t")
            S_.T(nc.tensor.matmul(p_rb8[0:8, :], s_o8[0:1, 1:9], s_rsb[:, :],
                                  start=True, stop=True))
            S_.wa("v")
            S_.V(nc.vector.tensor_mul(s_at[:, :], s_ed[:, :], p_rb8[0:8, :]))
            for g in range(4):
                S_.wa("t")
                S_.T(nc.tensor.matmul(p_abc[:, :], s_Eg[:, g * 128:(g + 1) * 128],
                                      s_at[:, :], start=True, stop=True))
                S_.wa("v")
                S_.V(nc.vector.scalar_tensor_tensor(s_u[g][:, :], p_abc[:, :],
                                                    s_dec[:, 4 + g:5 + g],
                                                    s_v[g][:, :],
                                                    ALU.mult, ALU.mult))
                S_.Vb(nc.vector.tensor_tensor_scan(s_stt[g][:, :], s_dbc[g][:, :],
                                                   s_u[g][:, :], s_fc[:, g:g + 1],
                                                   ALU.mult, ALU.add))
                S_.Vb(nc.vector.tensor_copy(s_fc[:, g:g + 1],
                                            s_stt[g][:, SP - 1:SP]))
                S_.wa("g")
                S_.st(u_fm[g * 128:(g + 1) * 128, t0:t0 + SP], s_u[g][:, :])
        S_.wa("g")
        S_.st(floc, s_fc[:, :])
        S_.finish()
    return nc


def build_B():
    nc = bass.Bass("TRN2", target_bir_lowering=False)
    u_fm = nc.dram_tensor("u_fm", [512, TC], F32, kind="ExternalInput").ap()
    q_fm = nc.dram_tensor("q_fm", [512, TC], BF16, kind="ExternalInput").ap()
    Wo = nc.dram_tensor("Wo", [512, 1024], F32, kind="ExternalInput").ap()
    colsB = nc.dram_tensor("colsB", [128, 20], F32, kind="ExternalInput").ap()
    gb = nc.dram_tensor("gb", [1, 2048], F32, kind="ExternalInput").ap()
    ident = nc.dram_tensor("ident", [128, 128], BF16, kind="ExternalInput").ap()
    onesr = nc.dram_tensor("onesr", [1, 512], BF16, kind="ExternalInput").ap()
    onesc = nc.dram_tensor("onesc", [128, 1], BF16, kind="ExternalInput").ap()
    s0_tm = nc.dram_tensor("s0_tm", [TC, 1024], BF16, kind="ExternalOutput").ap()

    SP, NSP = 512, TC // 512
    with contextlib.ExitStack() as st:
        s_u = [_sb(nc, st, f"s_u{g}", [128, SP], F32) for g in range(4)]
        s_q = [_sb(nc, st, f"s_q{g}", [128, SP], BF16) for g in range(4)]
        s_stt = [_sb(nc, st, f"s_st{g}", [128, SP], F32) for g in range(4)]
        s_aug = _sb(nc, st, "s_aug", [128, SP], F32)
        s_tmp = _sb(nc, st, "s_tmp", [128, SP], F32)
        s_mix = [_sb(nc, st, f"s_mx{g}", [128, SP], BF16) for g in range(4)]
        s_wo = [_sb(nc, st, f"s_wo{g}", [128, 1024], BF16) for g in range(4)]
        s_wld = _sb(nc, st, "s_wld", [128, 1024], F32)
        s_cols = _sb(nc, st, "s_cols", [128, 20], F32)
        s_dec = _sb(nc, st, "s_dec", [128, 4], F32)
        s_dbc = [_sb(nc, st, f"s_db{g}", [128, SP], F32) for g in range(4)]
        s_ones = _sb(nc, st, "s_ones", [128, SP], F32)
        s_fc = _sb(nc, st, "s_fc", [128, 4], F32)
        s_eps = _sb(nc, st, "s_eps", [128, 1], F32)
        s_o = [_sb(nc, st, f"s_o{oc}", [128, SP], BF16) for oc in range(8)]
        s_sq = _sb(nc, st, "s_sq", [128, SP], BF16)
        s_r1 = _sb(nc, st, "s_r1", [1, SP], F32)
        s_r2 = _sb(nc, st, "s_r2", [1, SP], F32)
        s_r3 = _sb(nc, st, "s_r3", [1, SP], F32)
        s_rstdb = _sb(nc, st, "s_rstdb", [1, SP], BF16)
        s_Bb = _sb(nc, st, "s_Bb", [1, SP], BF16)
        s_gbr = _sb(nc, st, "s_gbr", [1, 2048], BF16)
        s_gbl = _sb(nc, st, "s_gbl", [1, 2048], F32)
        s_onesr = _sb(nc, st, "s_onesr", [1, SP], BF16)
        s_onesc = _sb(nc, st, "s_onesc", [128, 1], BF16)
        s_sf = _sb(nc, st, "s_sf", [128, SP], F32)
        s_sfb = [_sb(nc, st, f"s_sfb{oc}", [128, SP], BF16) for oc in range(8)]
        s_stm = _sb(nc, st, "s_stm", [128, 1024], BF16)
        s_id = _sb(nc, st, "s_id", [128, 128], BF16)
        p_wo = _ps(nc, st, "p_wo", [128, 512], F32)
        p_s1 = _ps(nc, st, "p_s1", [128, 512], F32)
        p_s2 = _ps(nc, st, "p_s2", [128, 512], F32)
        p_A = _ps(nc, st, "p_A", [128, 512], F32)
        p_B = _ps(nc, st, "p_B", [128, 512], F32)
        p_tr = _ps(nc, st, "p_tr", [128, 1024], BF16)
        st.enter_context(nc.Block())
        S_ = Sch(nc, st)

        S_.ld(s_id[:, :], ident)
        S_.ld(s_cols[:, :], colsB)
        S_.ld(s_onesr[:, :], onesr)
        S_.ld(s_onesc[:, :], onesc)
        S_.ld(s_gbl[:, :], gb)
        for g in range(4):
            S_.wa("q")
            S_.ld(s_wld[:, :], Wo[g * 128:(g + 1) * 128, :])
            S_.wa("v")
            S_.V(nc.vector.tensor_copy(s_wo[g][:, :], s_wld[:, :]))
        S_.V(nc.vector.tensor_copy(s_gbr[:, :], s_gbl[:, :]))
        S_.V(nc.vector.memset(s_ones[:, :], 1.0))
        S_.V(nc.vector.memset(s_eps[:, :], 1e-6))
        S_.wa("s")
        for g in range(4):
            S_.S(nc.scalar.activation(s_dec[:, g:g + 1], s_cols[:, 4 + g:5 + g],
                                      AF.Sigmoid))
        S_.wa("v")
        for g in range(4):
            S_.V(nc.vector.tensor_scalar_mul(s_dbc[g][:, :], s_ones[:, :],
                                             s_dec[:, g:g + 1]))
            S_.Vb(nc.vector.tensor_copy(s_fc[:, g:g + 1], s_cols[:, g:g + 1]))

        for sp in range(NSP):
            t0 = sp * SP
            S_.wa("q")
            for g in range(4):
                S_.ld(s_u[g][:, :], u_fm[g * 128:(g + 1) * 128, t0:t0 + SP])
                S_.ld(s_q[g][:, :], q_fm[g * 128:(g + 1) * 128, t0:t0 + SP])
            S_.wa("v")
            S_.wa("s")
            for g in range(4):
                S_.Vb(nc.vector.tensor_tensor_scan(s_stt[g][:, :], s_dbc[g][:, :],
                                                   s_u[g][:, :], s_fc[:, g:g + 1],
                                                   ALU.mult, ALU.add))
                S_.Vb(nc.vector.tensor_copy(s_fc[:, g:g + 1],
                                            s_stt[g][:, SP - 1:SP]))
                S_.wa("s")
                S_.S(nc.scalar.activation(s_aug[:, :], s_q[g][:, :], AF.Sigmoid,
                                          scale=s_cols[:, 8 + g:9 + g]))
                S_.wa("v")
                S_.wa("s")
                S_.V(nc.vector.tensor_mul(s_tmp[:, :], s_q[g][:, :],
                                          s_stt[g][:, :]))
                S_.V(nc.vector.tensor_add(s_mix[g][:, :], s_tmp[:, :],
                                          s_aug[:, :]))
            for oc in range(8):
                S_.wa("t")
                for g in range(4):
                    S_.T(nc.tensor.matmul(p_wo[:, :],
                                          s_wo[g][:, oc * 128:(oc + 1) * 128],
                                          s_mix[g][:, :],
                                          start=(g == 0), stop=(g == 3)))
                S_.wa("v")
                S_.V(nc.vector.tensor_scalar(s_o[oc][:, :], p_wo[:, :],
                                             s_cols[:, 12 + oc:13 + oc], None,
                                             ALU.add))
            for oc in range(8):
                S_.wa("s")
                S_.S(nc.scalar.activation(s_sq[:, :], s_o[oc][:, :], AF.Square))
                S_.wa("t")
                S_.T(nc.tensor.matmul(p_s1[0:1, :], s_onesc[:, :], s_o[oc][:, :],
                                      start=(oc == 0), stop=(oc == 7)))
                S_.T(nc.tensor.matmul(p_s2[0:1, :], s_onesc[:, :], s_sq[:, :],
                                      start=(oc == 0), stop=(oc == 7)))
            S_.wa("v")
            S_.Vb(nc.vector.tensor_scalar_mul(s_r1[:, :], p_s1[0:1, :], 1.0 / 1024))
            S_.Vb(nc.vector.tensor_scalar_mul(s_r2[:, :], p_s2[0:1, :], 1.0 / 1024))
            S_.Vb(nc.vector.tensor_mul(s_r3[:, :], s_r1[:, :], s_r1[:, :]))
            S_.Vb(nc.vector.tensor_sub(s_r2[:, :], s_r2[:, :], s_r3[:, :]))
            S_.wa("s")
            S_.Sb(nc.scalar.activation(s_r2[:, :], s_r2[:, :], AF.Sqrt,
                                       bias=s_eps[0:1, 0:1]))
            S_.wa("v")
            S_.Vb(nc.vector.reciprocal(s_r2[:, :], s_r2[:, :]))
            S_.Vb(nc.vector.scalar_tensor_tensor(s_r3[:, :], s_r1[:, :], -1.0,
                                                 s_r2[:, :], ALU.mult, ALU.mult))
            S_.Vb(nc.vector.tensor_copy(s_rstdb[:, :], s_r2[:, :]))
            S_.Vb(nc.vector.tensor_copy(s_Bb[:, :], s_r3[:, :]))
            for oc in range(8):
                S_.wa("t")
                S_.T(nc.tensor.matmul(p_A[:, :],
                                      s_gbr[0:1, oc * 128:(oc + 1) * 128],
                                      s_rstdb[:, :], start=True, stop=True))
                S_.T(nc.tensor.matmul(p_B[:, :],
                                      s_gbr[0:1, oc * 128:(oc + 1) * 128],
                                      s_Bb[:, :], start=True, stop=False))
                S_.T(nc.tensor.matmul(p_B[:, :],
                                      s_gbr[0:1, 1024 + oc * 128:1024 + (oc + 1) * 128],
                                      s_onesr[:, :], start=False, stop=True))
                S_.wa("v")
                S_.V(nc.vector.tensor_mul(s_sf[:, :], s_o[oc][:, :], p_A[:, :]))
                S_.V(nc.vector.tensor_add(s_sfb[oc][:, :], s_sf[:, :], p_B[:, :]))
            _tr_fm_to_tm_store(S_, nc, s_sfb, p_tr, s_stm, s0_tm, t0, s_id, SP)
        S_.finish()
    return nc


def build_C():
    nc = bass.Bass("TRN2", target_bir_lowering=False)
    xn_fm = nc.dram_tensor("xn_fm", [1024, TC], BF16, kind="ExternalInput").ap()
    w0c = nc.dram_tensor("w0c", [1024, 4096], BF16, kind="ExternalInput").ap()
    w1b = nc.dram_tensor("w1b", [4096, 4096], BF16, kind="ExternalInput").ap()
    w2b = nc.dram_tensor("w2b", [4096, 1024], BF16, kind="ExternalInput").ap()
    s3 = nc.dram_tensor("s3", [1, 3], F32, kind="ExternalInput").ap()
    cst3 = nc.dram_tensor("cst3", [1, 384], BF16, kind="ExternalInput").ap()
    ident = nc.dram_tensor("ident", [128, 128], BF16, kind="ExternalInput").ap()
    g0_tm = nc.dram_tensor("g0_tm", [TC, 1024], BF16, kind="ExternalOutput").ap()

    SP, NSP = 1024, TC // 1024
    with contextlib.ExitStack() as st:
        s_xn = [_sb(nc, st, f"s_xn{k}", [128, SP], BF16) for k in range(8)]
        s_h1 = [_sb(nc, st, f"s_h1{k}", [128, SP], BF16) for k in range(32)]
        s_h2 = [_sb(nc, st, f"s_h2{k}", [128, SP], BF16) for k in range(32)]
        s_g = [_sb(nc, st, f"s_g{k}", [128, SP], BF16) for k in range(8)]
        s_p0 = [_sb(nc, st, f"s_p0{k}", [128, 512], BF16) for k in range(8)]
        s_p1 = [_sb(nc, st, f"s_p1{k}", [128, 512], BF16) for k in range(32)]
        s_s3f = _sb(nc, st, "s_s3f", [1, 3], F32)
        s_s3b = _sb(nc, st, "s_s3b", [1, 3], BF16)
        s_cst = _sb(nc, st, "s_cst", [1, 384], BF16)
        s_sc = _sb(nc, st, "s_sc", [128, 3], F32)
        s_stm = _sb(nc, st, "s_stm", [128, 1024], BF16)
        s_id = _sb(nc, st, "s_id", [128, 128], BF16)
        p_mm = _ps(nc, st, "p_mm", [128, 512], F32)
        p_sc = _ps(nc, st, "p_sc", [128, 512], F32)
        p_tr = _ps(nc, st, "p_tr", [128, 1024], BF16)
        st.enter_context(nc.Block())
        S_ = Sch(nc, st)

        S_.ld(s_id[:, :], ident)
        S_.ld(s_s3f[:, :], s3)
        S_.ld(s_cst[:, :], cst3)
        S_.wa("v")
        S_.V(nc.vector.tensor_copy(s_s3b[:, :], s_s3f[:, :]))
        for j in range(3):
            S_.wa("t")
            S_.T(nc.tensor.matmul(p_sc[:, 0:1], s_cst[0:1, j * 128:(j + 1) * 128],
                                  s_s3b[0:1, j:j + 1], start=True, stop=True))
            S_.wa("v")
            S_.V(nc.vector.tensor_copy(s_sc[:, j:j + 1], p_sc[:, 0:1]))

        for sp in range(NSP):
            t0 = sp * SP
            S_.wa("q")
            for k in range(8):
                S_.ld(s_xn[k][:, :], xn_fm[k * 128:(k + 1) * 128, t0:t0 + SP])
            for ocg in range(8):
                S_.wa("q")
                for kc in range(8):
                    S_.ld(s_p0[kc][:, :],
                          w0c[kc * 128:(kc + 1) * 128, ocg * 512:(ocg + 1) * 512])
                for oc4 in range(4):
                    oc = ocg * 4 + oc4
                    for hf in range(2):
                        S_.wa("t")
                        for kc in range(8):
                            S_.T(nc.tensor.matmul(
                                p_mm[:, :],
                                s_p0[kc][:, oc4 * 128:(oc4 + 1) * 128],
                                s_xn[kc][:, hf * 512:(hf + 1) * 512],
                                start=(kc == 0), stop=(kc == 7)))
                        S_.wa("s")
                        S_.S(nc.scalar.activation(
                            s_h1[oc][:, hf * 512:(hf + 1) * 512], p_mm[:, :],
                            AF.Relu, scale=s_sc[:, 0:1]))
            for ocg in range(8):
                S_.wa("q")
                for kc in range(32):
                    S_.ld(s_p1[kc][:, :],
                          w1b[kc * 128:(kc + 1) * 128, ocg * 512:(ocg + 1) * 512])
                for oc4 in range(4):
                    oc = ocg * 4 + oc4
                    for hf in range(2):
                        S_.wa("t")
                        for kc in range(32):
                            S_.T(nc.tensor.matmul(
                                p_mm[:, :],
                                s_p1[kc][:, oc4 * 128:(oc4 + 1) * 128],
                                s_h1[kc][:, hf * 512:(hf + 1) * 512],
                                start=(kc == 0), stop=(kc == 31)))
                        S_.wa("s")
                        S_.S(nc.scalar.activation(
                            s_h2[oc][:, hf * 512:(hf + 1) * 512], p_mm[:, :],
                            AF.Relu, scale=s_sc[:, 1:2]))
            for oc3 in range(8):
                S_.wa("q")
                for kc in range(32):
                    S_.ld(s_p1[kc][:, 0:128],
                          w2b[kc * 128:(kc + 1) * 128, oc3 * 128:(oc3 + 1) * 128])
                for hf in range(2):
                    S_.wa("t")
                    for kc in range(32):
                        S_.T(nc.tensor.matmul(
                            p_mm[:, :], s_p1[kc][:, 0:128],
                            s_h2[kc][:, hf * 512:(hf + 1) * 512],
                            start=(kc == 0), stop=(kc == 31)))
                    S_.wa("s")
                    S_.S(nc.scalar.activation(
                        s_g[oc3][:, hf * 512:(hf + 1) * 512], p_mm[:, :],
                        AF.Silu, scale=s_sc[:, 2:3]))
            _tr_fm_to_tm_store(S_, nc, s_g, p_tr, s_stm, g0_tm, t0, s_id, SP)
        S_.finish()
    return nc


def build_D():
    nc = bass.Bass("TRN2", target_bir_lowering=False)
    x_tm = nc.dram_tensor("x_tm", [TC, 1024], F32, kind="ExternalInput").ap()
    s0_tm = nc.dram_tensor("s0_tm", [TC, 1024], BF16, kind="ExternalInput").ap()
    g0_tm = nc.dram_tensor("g0_tm", [TC, 1024], BF16, kind="ExternalInput").ap()
    wq = nc.dram_tensor("wq", [1024, 1024], F32, kind="ExternalInput").ap()
    wkd = nc.dram_tensor("wkd", [1024, 64], F32, kind="ExternalInput").ap()
    wvd = nc.dram_tensor("wvd", [1024, 64], F32, kind="ExternalInput").ap()
    wkupT = nc.dram_tensor("wkupT", [128, 32], F32, kind="ExternalInput").ap()
    ident = nc.dram_tensor("ident", [128, 128], BF16, kind="ExternalInput").ap()
    x1_tm = nc.dram_tensor("x1_tm", [TC, 1024], F32, kind="ExternalOutput").ap()
    xn1_fm = nc.dram_tensor("xn1_fm", [1024, TC], BF16, kind="ExternalOutput").ap()
    klat = nc.dram_tensor("klat", [64, TC], BF16, kind="ExternalOutput").ap()
    vlat = nc.dram_tensor("vlat", [64, TC], BF16, kind="ExternalOutput").ap()
    qproj = nc.dram_tensor("qproj", [256, TC], BF16, kind="ExternalOutput").ap()

    SP, NSP = 512, TC // 512
    with contextlib.ExitStack() as st:
        s_x = [_sb(nc, st, f"s_x{i}", [128, 1024], F32) for i in range(4)]
        s_s0 = _sb(nc, st, "s_s0", [128, 1024], BF16)
        s_g0 = _sb(nc, st, "s_g0", [128, 1024], BF16)
        s_tmp = _sb(nc, st, "s_tmp", [128, 1024], F32)
        s_xn = [_sb(nc, st, f"s_xn{i}", [128, 1024], BF16) for i in range(4)]
        s_scr = _sb(nc, st, "s_scr", [128, 1024], F32)
        s_c = _sb(nc, st, "s_c", [128, 8], F32)
        s_xnf = [_sb(nc, st, f"s_xnf{j}", [128, SP], BF16) for j in range(8)]
        s_wq = [_sb(nc, st, f"s_wq{k}", [128, 1024], BF16) for k in range(8)]
        s_wkd = [_sb(nc, st, f"s_wkd{k}", [128, 64], BF16) for k in range(8)]
        s_wvd = [_sb(nc, st, f"s_wvd{k}", [128, 64], BF16) for k in range(8)]
        s_wup = _sb(nc, st, "s_wup", [128, 32], BF16)
        s_wldf = _sb(nc, st, "s_wldf", [128, 1024], F32)
        s_qh = _sb(nc, st, "s_qh", [128, SP], BF16)
        s_kl = _sb(nc, st, "s_kl", [64, SP], BF16)
        s_vl = _sb(nc, st, "s_vl", [64, SP], BF16)
        s_qp = _sb(nc, st, "s_qp", [32, SP], BF16)
        s_eps = _sb(nc, st, "s_eps", [128, 1], F32)
        s_id = _sb(nc, st, "s_id", [128, 128], BF16)
        p_tr = _ps(nc, st, "p_tr", [128, 1024], BF16)
        p_mm = _ps(nc, st, "p_mm", [128, 512], F32)
        p_lat = _ps(nc, st, "p_lat", [128, 512], F32)
        p_qp = _ps(nc, st, "p_qp", [128, 512], F32)
        st.enter_context(nc.Block())
        S_ = Sch(nc, st)

        S_.ld(s_id[:, :], ident)
        S_.V(nc.vector.memset(s_eps[:, :], 1e-5))
        for kc in range(8):
            S_.wa("q")
            S_.ld(s_wldf[:, :], wq[kc * 128:(kc + 1) * 128, :])
            S_.wa("v")
            S_.V(nc.vector.tensor_copy(s_wq[kc][:, :], s_wldf[:, :]))
            S_.wa("q")
            S_.ld(s_wldf[:, 0:64], wkd[kc * 128:(kc + 1) * 128, :])
            S_.ld(s_wldf[:, 64:128], wvd[kc * 128:(kc + 1) * 128, :])
            S_.wa("v")
            S_.V(nc.vector.tensor_copy(s_wkd[kc][:, :], s_wldf[:, 0:64]))
            S_.V(nc.vector.tensor_copy(s_wvd[kc][:, :], s_wldf[:, 64:128]))
        S_.wa("q")
        S_.ld(s_wldf[:, 0:32], wkupT)
        S_.wa("v")
        S_.V(nc.vector.tensor_scalar_mul(s_wup[:, :], s_wldf[:, 0:32], SCALE1))

        for sp in range(NSP):
            t0 = sp * SP
            for i in range(4):
                r0 = t0 + i * 128
                S_.wa("q")
                S_.ld(s_x[i][:, :], x_tm[r0:r0 + 128, :])
                S_.ld(s_s0[:, :], s0_tm[r0:r0 + 128, :])
                S_.ld(s_g0[:, :], g0_tm[r0:r0 + 128, :])
                S_.wa("v")
                S_.V(nc.vector.tensor_mul(s_tmp[:, :], s_s0[:, :], s_g0[:, :]))
                S_.V(nc.vector.tensor_add(s_x[i][:, :], s_x[i][:, :],
                                          s_tmp[:, :]))
                S_.wa("g")
                S_.st(x1_tm[r0:r0 + 128, :], s_x[i][:, :])
                S_.wa("s")
                _ln_tile(S_, nc, s_x[i][:, :], s_xn[i][:, :], s_scr[:, :],
                         s_c[:, 0:1], s_c[:, 1:2], s_c[:, 2:3], s_eps[:, 0:1],
                         1024)
            _tr_tm_to_fm(S_, nc, s_xn, s_xnf, p_tr, s_id, SP)
            S_.wa("g")
            for j in range(8):
                S_.st(xn1_fm[j * 128:(j + 1) * 128, t0:t0 + SP], s_xnf[j][:, :])
            S_.wa("t")
            for kc in range(8):
                S_.T(nc.tensor.matmul(p_lat[0:64, :], s_wkd[kc][:, :],
                                      s_xnf[kc][:, :], start=(kc == 0),
                                      stop=(kc == 7)))
            S_.wa("v")
            S_.V(nc.vector.tensor_copy(s_kl[:, :], p_lat[0:64, :]))
            S_.wa("t")
            for kc in range(8):
                S_.T(nc.tensor.matmul(p_lat[0:64, :], s_wvd[kc][:, :],
                                      s_xnf[kc][:, :], start=(kc == 0),
                                      stop=(kc == 7)))
            S_.wa("v")
            S_.V(nc.vector.tensor_copy(s_vl[:, :], p_lat[0:64, :]))
            S_.wa("g")
            S_.st(klat[:, t0:t0 + SP], s_kl[:, :])
            S_.st(vlat[:, t0:t0 + SP], s_vl[:, :])
            for H in range(8):
                S_.wa("t")
                for kc in range(8):
                    S_.T(nc.tensor.matmul(p_mm[:, :],
                                          s_wq[kc][:, H * 128:(H + 1) * 128],
                                          s_xnf[kc][:, :], start=(kc == 0),
                                          stop=(kc == 7)))
                S_.wa("v")
                S_.V(nc.vector.tensor_copy(s_qh[:, :], p_mm[:, :]))
                S_.wa("t")
                S_.T(nc.tensor.matmul(p_qp[0:32, :], s_wup[:, :], s_qh[:, :],
                                      start=True, stop=True))
                S_.wa("v")
                S_.V(nc.vector.tensor_copy(s_qp[:, :], p_qp[0:32, :]))
                S_.wa("g")
                S_.st(qproj[H * 32:(H + 1) * 32, t0:t0 + SP], s_qp[:, :])
        S_.finish()
    return nc


def build_E():
    nc = bass.Bass("TRN2", target_bir_lowering=False)
    qpE = nc.dram_tensor("qpE", [256, TC], BF16, kind="ExternalInput").ap()
    klat0 = nc.dram_tensor("klat0", [32, S], BF16, kind="ExternalInput").ap()
    klat1 = nc.dram_tensor("klat1", [32, S], BF16, kind="ExternalInput").ap()
    vlat66 = nc.dram_tensor("vlat66", [S, 66], BF16, kind="ExternalInput").ap()
    wvup = nc.dram_tensor("wvup", [32, 128], F32, kind="ExternalInput").ap()
    wo = nc.dram_tensor("wo", [1024, 1024], F32, kind="ExternalInput").ap()
    emask = nc.dram_tensor("emask", [8192, 512], BF16, kind="ExternalInput").ap()
    o1r = nc.dram_tensor("o1r", [1, 128], BF16, kind="ExternalInput").ap()
    ident = nc.dram_tensor("ident", [128, 128], BF16, kind="ExternalInput").ap()
    s1_tm = nc.dram_tensor("s1_tm", [TC, 1024], BF16, kind="ExternalOutput").ap()

    with contextlib.ExitStack() as st:
        s_kl = [_sb(nc, st, f"s_kl{v}", [32, S], BF16) for v in range(2)]
        s_vl = _sb(nc, st, "s_vl", [128, 64 * 66], BF16)
        s_qp = [_sb(nc, st, f"s_qp{H}", [32, 512], BF16) for H in range(8)]
        s_m = [_sb(nc, st, f"s_m{j}", [128, 512], BF16) for j in range(8)]
        s_pt = [_sb(nc, st, f"s_pt{j}", [128, 512], BF16) for j in range(2)]
        s_rd = _sb(nc, st, "s_rd", [1, 512], F32)
        s_rdb = _sb(nc, st, "s_rdb", [1, 512], BF16)
        s_rb = _sb(nc, st, "s_rb", [32, 512], F32)
        s_yl = _sb(nc, st, "s_yl", [32, 512], BF16)
        s_y = [_sb(nc, st, f"s_y{H}", [128, 512], BF16) for H in range(8)]
        s_wvupb = _sb(nc, st, "s_wvupb", [32, 128], BF16)
        s_wo = [_sb(nc, st, f"s_wo{k}", [128, 1024], BF16) for k in range(8)]
        s_wldf = _sb(nc, st, "s_wldf", [128, 1024], F32)
        s_o1r = _sb(nc, st, "s_o1r", [1, 128], BF16)
        s_sf = [_sb(nc, st, f"s_sf{oc}", [128, 512], BF16) for oc in range(8)]
        s_stm = _sb(nc, st, "s_stm", [128, 1024], BF16)
        s_id = _sb(nc, st, "s_id", [128, 128], BF16)
        p_sc = [_ps(nc, st, f"p_sc{j}", [128, 512], F32) for j in range(2)]
        p_yl = _ps(nc, st, "p_yl", [128, 512], F32)
        p_rb = _ps(nc, st, "p_rb", [128, 512], F32)
        p_y = _ps(nc, st, "p_y", [128, 512], F32)
        p_s1 = _ps(nc, st, "p_s1", [128, 512], F32)
        p_tr = _ps(nc, st, "p_tr", [128, 1024], BF16)
        st.enter_context(nc.Block())
        S_ = Sch(nc, st)

        S_.ld(s_id[:, :], ident)
        S_.ld(s_o1r[:, :], o1r)
        S_.ld(s_kl[0][:, :], klat0)
        S_.ld(s_kl[1][:, :], klat1)
        for kb in range(64):
            S_.ld(s_vl[:, kb * 66:(kb + 1) * 66],
                  vlat66[kb * 128:(kb + 1) * 128, :])
        S_.wa("q")
        S_.ld(s_wldf[0:32, 0:128], wvup)
        S_.wa("v")
        S_.V(nc.vector.tensor_copy(s_wvupb[:, :], s_wldf[0:32, 0:128]))
        for kc in range(8):
            S_.wa("q")
            S_.ld(s_wldf[:, :], wo[kc * 128:(kc + 1) * 128, :])
            S_.wa("v")
            S_.V(nc.vector.tensor_copy(s_wo[kc][:, :], s_wldf[:, :]))

        for l in range(8):
            nkb = 8 * l + 8
            S_.wa("q")
            for H in range(8):
                S_.ld(s_qp[H][:, :], qpE[H * 32:(H + 1) * 32,
                                         l * 512:(l + 1) * 512])
            for j in range(8):
                S_.ld(s_m[j][:, :], emask[(8 * l + j) * 128:(8 * l + j + 1) * 128, :])
            for H in range(8):
                kv = H // 4
                for kb in range(nkb):
                    pb = kb % 2
                    S_.wa("t")
                    S_.T(nc.tensor.matmul(p_sc[pb][:, :],
                                          s_kl[kv][:, kb * 128:(kb + 1) * 128],
                                          s_qp[H][:, :], start=True, stop=True))
                    S_.wa("s")
                    S_.S(nc.scalar.activation(s_pt[pb][:, :], p_sc[pb][:, :],
                                              AF.Exp))
                    if kb >= 8 * l:
                        S_.wa("v")
                        S_.V(nc.vector.tensor_mul(s_pt[pb][:, :], s_pt[pb][:, :],
                                                  s_m[kb - 8 * l][:, :]))
                    S_.wa("t")
                    S_.T(nc.tensor.matmul(p_yl[0:33, :],
                                          s_vl[:, kb * 66 + 33 * kv:
                                               kb * 66 + 33 * kv + 33],
                                          s_pt[pb][:, :], start=(kb == 0),
                                          stop=(kb == nkb - 1)))
                S_.wa("v")
                S_.V(nc.vector.reciprocal(s_rd[:, :], p_yl[32:33, :]))
                S_.V(nc.vector.tensor_copy(s_rdb[:, :], s_rd[:, :]))
                S_.wa("t")
                S_.T(nc.tensor.matmul(p_rb[0:32, :], s_o1r[0:1, 0:32],
                                      s_rdb[:, :], start=True, stop=True))
                S_.wa("v")
                S_.V(nc.vector.tensor_copy(s_rb[:, :], p_rb[0:32, :]))
                S_.V(nc.vector.tensor_mul(s_yl[:, :], p_yl[0:32, :], s_rb[:, :]))
                S_.wa("t")
                S_.T(nc.tensor.matmul(p_y[:, :], s_wvupb[:, :], s_yl[:, :],
                                      start=True, stop=True))
                S_.wa("v")
                S_.V(nc.vector.tensor_copy(s_y[H][:, :], p_y[:, :]))
            for oc in range(8):
                S_.wa("t")
                for H in range(8):
                    S_.T(nc.tensor.matmul(p_s1[:, :],
                                          s_wo[H][:, oc * 128:(oc + 1) * 128],
                                          s_y[H][:, :], start=(H == 0),
                                          stop=(H == 7)))
                S_.wa("v")
                S_.V(nc.vector.tensor_copy(s_sf[oc][:, :], p_s1[:, :]))
            _tr_fm_to_tm_store(S_, nc, s_sf, p_tr, s_stm, s1_tm, l * 512, s_id,
                               512)
        S_.finish()
    return nc


def build_F():
    nc = bass.Bass("TRN2", target_bir_lowering=False)
    xn1_fm = nc.dram_tensor("xn1_fm", [1024, TC], BF16, kind="ExternalInput").ap()
    wgs = nc.dram_tensor("wgs", [1024, 3072], BF16, kind="ExternalInput").ap()
    wus = nc.dram_tensor("wus", [1024, 3072], BF16, kind="ExternalInput").ap()
    wds = nc.dram_tensor("wds", [3072, 1024], BF16, kind="ExternalInput").ap()
    x1_tm = nc.dram_tensor("x1_tm", [TC, 1024], F32, kind="ExternalInput").ap()
    s1_tm = nc.dram_tensor("s1_tm", [TC, 1024], BF16, kind="ExternalInput").ap()
    ident = nc.dram_tensor("ident", [128, 128], BF16, kind="ExternalInput").ap()
    x2_tm = nc.dram_tensor("x2_tm", [TC, 1024], F32, kind="ExternalOutput").ap()

    SP, NSP = 512, TC // 512
    with contextlib.ExitStack() as st:
        s_wg = [_sb(nc, st, f"s_wg{k}", [128, 3072], BF16) for k in range(8)]
        s_wu = [_sb(nc, st, f"s_wu{k}", [128, 3072], BF16) for k in range(8)]
        s_wd = [_sb(nc, st, f"s_wd{k}", [128, 1024], BF16) for k in range(24)]
        s_xn = [_sb(nc, st, f"s_xn{k}", [128, SP], BF16) for k in range(8)]
        s_sg = _sb(nc, st, "s_sg", [128, SP], BF16)
        s_h = [_sb(nc, st, f"s_h{k}", [128, SP], BF16) for k in range(24)]
        s_g1 = [_sb(nc, st, f"s_g1{k}", [128, SP], BF16) for k in range(8)]
        s_x1 = _sb(nc, st, "s_x1", [128, 1024], F32)
        s_s1 = _sb(nc, st, "s_s1", [128, 1024], BF16)
        s_gt = _sb(nc, st, "s_gt", [128, 1024], BF16)
        s_tmp = _sb(nc, st, "s_tmp", [128, 1024], F32)
        s_id = _sb(nc, st, "s_id", [128, 128], BF16)
        p_g = _ps(nc, st, "p_g", [128, 512], F32)
        p_u = _ps(nc, st, "p_u", [128, 512], F32)
        p_d = _ps(nc, st, "p_d", [128, 512], F32)
        p_tr = _ps(nc, st, "p_tr", [128, 1024], BF16)
        st.enter_context(nc.Block())
        S_ = Sch(nc, st)

        S_.ld(s_id[:, :], ident)
        for kc in range(8):
            S_.ld(s_wg[kc][:, :], wgs[kc * 128:(kc + 1) * 128, :])
            S_.ld(s_wu[kc][:, :], wus[kc * 128:(kc + 1) * 128, :])
        for kc in range(24):
            S_.ld(s_wd[kc][:, :], wds[kc * 128:(kc + 1) * 128, :])

        for sp in range(NSP):
            t0 = sp * SP
            S_.wa("q")
            for k in range(8):
                S_.ld(s_xn[k][:, :], xn1_fm[k * 128:(k + 1) * 128, t0:t0 + SP])
            S_.wa("v")
            for oc in range(24):
                S_.wa("t")
                for kc in range(8):
                    S_.T(nc.tensor.matmul(p_g[:, :],
                                          s_wg[kc][:, oc * 128:(oc + 1) * 128],
                                          s_xn[kc][:, :], start=(kc == 0),
                                          stop=(kc == 7)))
                for kc in range(8):
                    S_.T(nc.tensor.matmul(p_u[:, :],
                                          s_wu[kc][:, oc * 128:(oc + 1) * 128],
                                          s_xn[kc][:, :], start=(kc == 0),
                                          stop=(kc == 7)))
                S_.wa("s")
                S_.S(nc.scalar.activation(s_sg[:, :], p_g[:, :], AF.Silu,
                                          scale=DEQ))
                S_.wa("v")
                S_.V(nc.vector.scalar_tensor_tensor(s_h[oc][:, :], p_u[:, :],
                                                    DEQ, s_sg[:, :],
                                                    ALU.mult, ALU.mult))
            for oc3 in range(8):
                S_.wa("t")
                for kc in range(24):
                    S_.T(nc.tensor.matmul(p_d[:, :],
                                          s_wd[kc][:, oc3 * 128:(oc3 + 1) * 128],
                                          s_h[kc][:, :], start=(kc == 0),
                                          stop=(kc == 23)))
                S_.wa("s")
                S_.S(nc.scalar.activation(s_g1[oc3][:, :], p_d[:, :], AF.Silu,
                                          scale=DEQ))
            for i in range(4):
                S_.wa("t")
                for oc in range(8):
                    S_.T(nc.tensor.transpose(p_tr[:, oc * 128:(oc + 1) * 128],
                                             s_g1[oc][:, i * 128:(i + 1) * 128],
                                             s_id[:, :]))
                r0 = t0 + i * 128
                S_.wa("q")
                S_.ld(s_x1[:, :], x1_tm[r0:r0 + 128, :])
                S_.ld(s_s1[:, :], s1_tm[r0:r0 + 128, :])
                S_.wa("v")
                S_.V(nc.vector.tensor_copy(s_gt[:, :], p_tr[:, :]))
                S_.V(nc.vector.tensor_mul(s_tmp[:, :], s_s1[:, :], s_gt[:, :]))
                S_.V(nc.vector.tensor_add(s_tmp[:, :], s_x1[:, :], s_tmp[:, :]))
                S_.wa("g")
                S_.st(x2_tm[r0:r0 + 128, :], s_tmp[:, :])
        S_.finish()
    return nc


def _eye_bf():
    return np.eye(128, dtype=BF)


def _run(name, build_fn, in_maps):
    nc = _cached(name, build_fn)
    return run_bass_kernel_spmd(nc, in_maps, core_ids=list(range(NCORES)))


def _consts_A():
    segs = np.zeros((128, 32), dtype=BF)
    Eg = np.zeros((8, 512), dtype=BF)
    for g in range(4):
        for hh in range(2):
            h = 2 * g + hh
            segs[64 * hh:64 * (hh + 1), 8 * g + h] = 1
            Eg[h, g * 128 + 64 * hh:g * 128 + 64 * (hh + 1)] = 1
    ones8 = np.zeros((8, 9), dtype=BF)
    ones8[:, 0] = 1
    ones8[0, 1:9] = 1
    return segs, Eg, ones8


def _build_emask(parity):
    m = np.zeros((64, 128, 512), dtype=BF)
    for l in range(8):
        q0 = 512 * (2 * l + parity)
        for jj in range(8):
            k0 = 128 * (8 * l + jj)
            kk = k0 + np.arange(128)[:, None]
            qq = q0 + np.arange(512)[None, :]
            m[8 * l + jj] = (kk <= qq).astype(BF)
    return m.reshape(8192, 512)


def kernel(_collect=None, **inputs):
    ident = _eye_bf()
    f32 = np.float32

    x = np.asarray(inputs["L0_attn_x"], f32)
    x_sl = []
    for c in range(NCORES):
        b, h = c // 2, c % 2
        x_sl.append(np.ascontiguousarray(x[b, h * TC:(h + 1) * TC, :]))

    k0 = np.ascontiguousarray(np.asarray(inputs["L0_ffn_k0"])).view(np.uint8)
    k1 = np.ascontiguousarray(np.asarray(inputs["L0_ffn_k1"])).view(np.uint8)
    k2 = np.ascontiguousarray(np.asarray(inputs["L0_ffn_k2"])).view(np.uint8)
    wgT = np.ascontiguousarray(np.asarray(inputs["L1_ffn_w_gate"]).T)
    wuT = np.ascontiguousarray(np.asarray(inputs["L1_ffn_w_up"]).T)
    wdT = np.ascontiguousarray(np.asarray(inputs["L1_ffn_w_down"]).T)
    inW = []
    for c in range(NCORES):
        inW.append({
            "k0t": np.ascontiguousarray(k0[c * 128:(c + 1) * 128]),
            "k0b": np.ascontiguousarray(k0[1024 + c * 128:1024 + (c + 1) * 128]),
            "k1s": np.ascontiguousarray(k1[c * 512:(c + 1) * 512]),
            "k2s": np.ascontiguousarray(k2[c * 512:(c + 1) * 512]),
            "wgTs": np.ascontiguousarray(wgT[c * 128:(c + 1) * 128]),
            "wuTs": np.ascontiguousarray(wuT[c * 128:(c + 1) * 128]),
            "wdTs": np.ascontiguousarray(wdT[c * 384:(c + 1) * 384]),
        })
    rW = _run("W", build_W, inW).results
    if _collect is not None:
        _collect["W"] = rW
        if _collect.get("stop") == "W":
            return None
    w0c = np.concatenate([np.asarray(rW[c]["w0cs"]) for c in range(8)], 0)
    w1b = np.concatenate([np.asarray(rW[c]["w1s"]) for c in range(8)], 0)
    w2b = np.concatenate([np.asarray(rW[c]["w2s"]) for c in range(8)], 0)
    wgs = np.concatenate([np.asarray(rW[c]["wgs"]) for c in range(8)], 0)
    wus = np.concatenate([np.asarray(rW[c]["wus"]) for c in range(8)], 0)
    wds = np.concatenate([np.asarray(rW[c]["wds"]) for c in range(8)], 0)

    segs, Eg, ones8 = _consts_A()
    colsA = np.zeros((128, 16), f32)
    for g in range(4):
        colsA[:, 0 + g] = np.asarray(inputs["L0_attn_bq"], f32)[g * 128:(g + 1) * 128]
        colsA[:, 4 + g] = np.asarray(inputs["L0_attn_bk"], f32)[g * 128:(g + 1) * 128]
        colsA[:, 8 + g] = np.asarray(inputs["L0_attn_bv"], f32)[g * 128:(g + 1) * 128]
        colsA[:, 12 + g] = np.asarray(inputs["L0_attn_alpha"], f32).reshape(512)[g * 128:(g + 1) * 128]
    shared_A = {"Wq": np.asarray(inputs["L0_attn_Wq"], f32),
                "Wk": np.asarray(inputs["L0_attn_Wk"], f32),
                "Wv": np.asarray(inputs["L0_attn_Wv"], f32),
                "colsA": colsA, "ident": ident, "segs": segs, "Eg": Eg,
                "ones8": ones8}
    inA = [dict(shared_A, x_tm=x_sl[c]) for c in range(NCORES)]
    rA = _run("A", build_A, inA).results
    if _collect is not None:
        _collect["A"] = rA
        if _collect.get("stop") == "A":
            return None

    colsB_base = np.zeros((128, 20), f32)
    for g in range(4):
        colsB_base[:, 4 + g] = colsA[:, 12 + g]
        colsB_base[:, 8 + g] = np.asarray(inputs["L0_attn_aug_param"], f32).reshape(512)[g * 128:(g + 1) * 128]
    for oc in range(8):
        colsB_base[:, 12 + oc] = np.asarray(inputs["L0_attn_bo"], f32)[oc * 128:(oc + 1) * 128]
    gb = np.concatenate([np.asarray(inputs["L0_attn_ln_scale"], f32),
                         np.asarray(inputs["L0_attn_ln_bias"], f32)]).reshape(1, 2048)
    onesr = np.ones((1, 512), BF)
    onesc = np.ones((128, 1), BF)
    inB = []
    for c in range(NCORES):
        colsB = colsB_base.copy()
        if c % 2 == 1:
            colsB[:, 0:4] = np.asarray(rA[c - 1]["floc"], f32)
        inB.append({"u_fm": np.asarray(rA[c]["u_fm"]),
                    "q_fm": np.asarray(rA[c]["q_fm"]),
                    "Wo": np.asarray(inputs["L0_attn_Wo"], f32),
                    "colsB": colsB, "gb": gb, "ident": ident,
                    "onesr": onesr, "onesc": onesc})
    rB = _run("B", build_B, inB).results
    if _collect is not None:
        _collect["B"] = rB
        if _collect.get("stop") == "B":
            return None

    s3 = np.array([[np.asarray(inputs["L0_ffn_s0"], f32),
                    np.asarray(inputs["L0_ffn_s1"], f32),
                    np.asarray(inputs["L0_ffn_s2"], f32)]], f32).reshape(1, 3)
    cst3 = np.zeros((1, 384), BF)
    cst3[0, 0:128] = 4.0
    cst3[0, 128:256] = 2.0
    cst3[0, 256:384] = 1.0
    shared_C = {"w0c": w0c, "w1b": w1b, "w2b": w2b, "s3": s3, "cst3": cst3,
                "ident": ident}
    inC = [dict(shared_C, xn_fm=np.asarray(rA[c]["xn_fm"])) for c in range(NCORES)]
    rC = _run("C", build_C, inC).results
    if _collect is not None:
        _collect["C"] = rC
        if _collect.get("stop") == "C":
            return None

    shared_D = {"wq": np.asarray(inputs["L1_attn_wq"], f32),
                "wkd": np.asarray(inputs["L1_attn_wk_down"], f32),
                "wvd": np.asarray(inputs["L1_attn_wv_down"], f32),
                "wkupT": np.ascontiguousarray(np.asarray(inputs["L1_attn_wk_up"], f32).T),
                "ident": ident}
    inD = [dict(shared_D, x_tm=x_sl[c], s0_tm=np.asarray(rB[c]["s0_tm"]),
                g0_tm=np.asarray(rC[c]["g0_tm"])) for c in range(NCORES)]
    rD = _run("D", build_D, inD).results
    if _collect is not None:
        _collect["D"] = rD
        if _collect.get("stop") == "D":
            return None

    o1r = np.ones((1, 128), BF)
    inE = []
    for c in range(NCORES):
        b, p = c // 2, c % 2
        cA, cB = 2 * b, 2 * b + 1
        klat_seq = np.concatenate([np.asarray(rD[cA]["klat"]),
                                   np.asarray(rD[cB]["klat"])], 1)
        vlat_seq = np.concatenate([np.asarray(rD[cA]["vlat"]),
                                   np.asarray(rD[cB]["vlat"])], 1)
        vl66 = np.zeros((S, 66), BF)
        vl66[:, 0:32] = vlat_seq[0:32].T
        vl66[:, 32] = 1
        vl66[:, 33:65] = vlat_seq[32:64].T
        vl66[:, 65] = 1
        qpE = np.zeros((256, TC), BF)
        for l in range(8):
            tau = 2 * l + p
            owner, loc = (cA, tau) if tau < 8 else (cB, tau - 8)
            qpE[:, l * 512:(l + 1) * 512] = \
                np.asarray(rD[owner]["qproj"])[:, loc * 512:(loc + 1) * 512]
        inE.append({"qpE": qpE,
                    "klat0": np.ascontiguousarray(klat_seq[0:32]),
                    "klat1": np.ascontiguousarray(klat_seq[32:64]),
                    "vlat66": vl66,
                    "wvup": np.asarray(inputs["L1_attn_wv_up"], f32),
                    "wo": np.asarray(inputs["L1_attn_wo"], f32),
                    "emask": _build_emask(p), "o1r": o1r, "ident": ident})
    rE = _run("E", build_E, inE).results
    if _collect is not None:
        _collect["E"] = rE
        if _collect.get("stop") == "E":
            return None

    inF = []
    for c in range(NCORES):
        b, h = c // 2, c % 2
        s1 = np.zeros((TC, 1024), BF)
        for j in range(8):
            tau = 8 * h + j
            src_core = 2 * b + (tau % 2)
            l = tau // 2
            s1[j * 512:(j + 1) * 512] = \
                np.asarray(rE[src_core]["s1_tm"])[l * 512:(l + 1) * 512]
        inF.append({"xn1_fm": np.asarray(rD[c]["xn1_fm"]),
                    "wgs": wgs, "wus": wus, "wds": wds,
                    "x1_tm": np.asarray(rD[c]["x1_tm"]),
                    "s1_tm": s1, "ident": ident})
    rF = _run("F", build_F, inF).results
    if _collect is not None:
        _collect["F"] = rF

    out = np.empty((B, S, D), f32)
    for c in range(NCORES):
        b, h = c // 2, c % 2
        out[b, h * TC:(h + 1) * TC, :] = np.asarray(rF[c]["x2_tm"], f32)
    return out
